# version 13
# baseline (speedup 1.0000x reference)
import math
import contextlib
import numpy as np
import ml_dtypes
import concourse.bass as bass
import concourse.mybir as mybir
from concourse.bass_utils import run_bass_kernel_spmd

F32 = mybir.dt.float32
BF16 = mybir.dt.bfloat16
AF = mybir.ActivationFunctionType
ALU = mybir.AluOpType
AX = mybir.AxisListType

D = 2048
S = 2048
T = 4096
FF = 8192
NH = 16
DH = 128
NCD = 16
EPS = 1e-5
INV_SQRT_DH = 1.0 / math.sqrt(DH)
HALF_LN_DH = 0.5 * math.log(DH)


class Builder:

    def __init__(self, nc):
        self.nc = nc
        self.ops = {"sync": [], "tensor": [], "vector": [], "scalar": [], "gpsimd": []}
        self.cnt = {}
        self.sem = {}

    def make_sems(self, names, stack):
        for n in names:
            self.sem[n] = stack.enter_context(self.nc.semaphore("sem_" + n))
            self.cnt[n] = 0

    def op(self, engine, fn, waits=(), inc=None, amt=1):
        waits = [w for w in waits if w is not None]
        if inc is not None:
            self.cnt[inc] += amt
        self.ops[engine].append((tuple(waits), fn, inc, amt))
        return self.cnt[inc] if inc is not None else None

    def t(self, fn, waits=()):
        return self.op("tensor", fn, waits, inc="pe")

    def v(self, fn, waits=()):
        return self.op("vector", fn, waits, inc="ve")

    def s(self, fn, waits=()):
        return self.op("scalar", fn, waits, inc="se")

    def dma(self, queue, out_ap, in_ap, waits=(), transpose=False, sem=None):
        qsem = sem or {"sync": "sq", "scalar": "aq", "gpsimd": "pq"}[queue]
        if transpose:
            fn = lambda e, o=out_ap, i=in_ap: e.dma_start(o, i, transpose=True)
        else:
            fn = lambda e, o=out_ap, i=in_ap: e.dma_start(o, i)
        return self.op(queue, fn, waits=waits, inc=qsem, amt=16)

    def emit(self):
        b = self

        def mk(engine):
            def run(eng):
                for waits, fn, inc, amt in b.ops[engine]:
                    for (sname, val) in waits:
                        if val > 0:
                            eng.wait_ge(b.sem[sname], val)
                    inst = fn(eng)
                    if inc is not None:
                        inst.then_inc(b.sem[inc], amt)
            return run

        with self.nc.Block() as block:
            block.sync(mk("sync"))
            block.tensor(mk("tensor"))
            block.vector(mk("vector"))
            block.scalar(mk("scalar"))
            block.gpsimd(mk("gpsimd"))


def build_program(debug=()):
    nc = bass.Bass("TRN2", target_bir_lowering=False)

    x_in = nc.dram_tensor("x_in", [T, D], F32, kind="ExternalInput").ap()
    wq = nc.dram_tensor("wq", [D, D], F32, kind="ExternalInput").ap()
    wk = nc.dram_tensor("wk", [D, D], F32, kind="ExternalInput").ap()
    wv = nc.dram_tensor("wv", [D, D], F32, kind="ExternalInput").ap()
    wo = nc.dram_tensor("wo", [D, D], F32, kind="ExternalInput").ap()
    wg = nc.dram_tensor("wg", [FF, D], F32, kind="ExternalInput").ap()
    wu = nc.dram_tensor("wu", [FF, D], F32, kind="ExternalInput").ap()
    wd = nc.dram_tensor("wd", [D, FF], F32, kind="ExternalInput").ap()
    g16 = nc.dram_tensor("g16", [128, NCD], F32, kind="ExternalInput").ap()
    b16 = nc.dram_tensor("b16", [128, NCD], F32, kind="ExternalInput").ap()
    bq16 = nc.dram_tensor("bq16", [128, NCD], F32, kind="ExternalInput").ap()
    bk16 = nc.dram_tensor("bk16", [128, NCD], F32, kind="ExternalInput").ap()
    bv16 = nc.dram_tensor("bv16", [128, NCD], F32, kind="ExternalInput").ap()
    bo_row = nc.dram_tensor("bo_row", [1, D], F32, kind="ExternalInput").ap()
    ones_tile_in = nc.dram_tensor("ones_tile", [128, 128], BF16, kind="ExternalInput").ap()
    ones_col_in = nc.dram_tensor("ones_col", [128, 1], BF16, kind="ExternalInput").ap()
    ones_row_in = nc.dram_tensor("ones_row_f32", [1, 128], F32, kind="ExternalInput").ap()
    masks_in = nc.dram_tensor("masks", [512, 512], BF16, kind="ExternalInput").ap()
    fconst_in = nc.dram_tensor("fconst", [128, 4], F32, kind="ExternalInput").ap()

    out = nc.dram_tensor("out", [T, D], F32, kind="ExternalOutput").ap()

    def kindof(name):
        return "ExternalOutput" if name in debug else "Internal"
    xn_h = nc.dram_tensor("xn_h", [T, D], BF16, kind=kindof("xn_h")).ap()
    qT_h = nc.dram_tensor("qT_h", [2 * D, S], BF16, kind=kindof("qT_h")).ap()
    kT_h = nc.dram_tensor("kT_h", [2 * D, S], BF16, kind=kindof("kT_h")).ap()
    vT_h = nc.dram_tensor("vT_h", [2 * D, S], BF16, kind=kindof("vT_h")).ap()
    avT_h = nc.dram_tensor("avT_h", [2 * D, S], BF16, kind=kindof("avT_h")).ap()
    attnT_h = nc.dram_tensor("attnT_h", [2 * D, S], BF16, kind=kindof("attnT_h")).ap()
    x1_h = nc.dram_tensor("x1_h", [T, D], F32, kind=kindof("x1_h")).ap()
    xn2_h = nc.dram_tensor("xn2_h", [T, D], BF16, kind=kindof("xn2_h")).ap()
    wg_bf = nc.dram_tensor("wg_bf", [FF, D], BF16, kind=kindof("wg_bf")).ap()
    wu_bf = nc.dram_tensor("wu_bf", [FF, D], BF16, kind=kindof("wu_bf")).ap()
    wd_bf = nc.dram_tensor("wd_bf", [D, FF], BF16, kind=kindof("wd_bf")).ap()
    wgT_h = nc.dram_tensor("wgT_h", [D, FF], BF16, kind=kindof("wgT_h")).ap()
    xnT_dbg = (nc.dram_tensor("xnT_dbg", [2, 128, NCD, 2048], BF16,
                              kind="ExternalOutput").ap()
               if "xnT_dbg" in debug else None)
    wuT_h = nc.dram_tensor("wuT_h", [D, FF], BF16, kind=kindof("wuT_h")).ap()
    wdT_h = nc.dram_tensor("wdT_h", [FF, D], BF16, kind=kindof("wdT_h")).ap()
    gatedT_h = nc.dram_tensor("gatedT_h", [FF, T], BF16, kind=kindof("gatedT_h")).ap()

    stack = contextlib.ExitStack()
    b = Builder(nc)
    b.make_sems(["sq", "aq", "pq", "pe", "ve", "se",
                 "mis", "cst", "cwg", "cwu", "cwd", "wb0s", "wb1s",
                 "et0", "et1", "av0", "av1", "xf",
                 "xn0", "xn1", "gt0", "gt1", "outs"], stack)

    stack2 = contextlib.ExitStack()

    def sb(name, shape, dtype, stk=None):
        return (stk or stack).enter_context(nc.sbuf_tensor(name, shape, dtype))

    bigA = sb("bigA", [128, NCD, 2048], BF16)
    bigB = sb("bigB", [128, NCD, 2048], BF16)
    stats = sb("stats", [128, 24], F32)
    hstats = sb("hstats", [128, 48], F32)
    xa = sb("xa", [128, D], F32)
    xb = sb("xb", [128, D], F32)
    t0 = sb("t0", [128, D], BF16)
    t1 = sb("t1", [128, D], BF16)
    t2 = sb("t2", [128, D], BF16)
    f0 = sb("f0", [128, D], F32)
    sqscr = sb("sqscr", [128, D], BF16)
    fconst = sb("fconst_s", [128, 4], F32)
    g16s = sb("g16s", [128, NCD], F32, stack2)
    b16s = sb("b16s", [128, NCD], F32, stack2)
    bq16s = sb("bq16s", [128, NCD], F32, stack2)
    bk16s = sb("bk16s", [128, NCD], F32, stack2)
    bv16s = sb("bv16s", [128, NCD], F32, stack2)
    bo_tile = sb("bo_tile", [128, D], BF16, stack2)
    bo_row_s = sb("bo_row_s", [1, D], BF16, stack2)
    ones_tile = sb("ones_tile_s", [128, 128], BF16, stack2)
    ones_col = sb("ones_col_s", [128, 1], BF16, stack2)
    ones_row = sb("ones_row_s", [1, 128], F32, stack2)
    masks = sb("masks_s", [128, 4, 512], BF16, stack2)
    rows = sb("rows", [1, 1024], F32, stack2)
    wb0 = sb("wb0", [128, NCD, 128], BF16, stack2)
    wb1 = sb("wb1", [128, NCD, 128], BF16, stack2)
    rt0 = sb("rt0", [128, 512], F32, stack2)
    rt1 = sb("rt1", [128, 512], F32, stack2)
    pb0 = sb("pb0", [128, 2048], BF16, stack2)
    pb1 = sb("pb1", [128, 2048], BF16, stack2)
    ev0 = sb("ev0", [128, 512], BF16, stack2)
    ev1 = sb("ev1", [128, 512], BF16, stack2)
    accum = None

    psA = stack.enter_context(nc.psum_tensor("psA", [128, 2048], F32))
    psB = stack.enter_context(nc.psum_tensor("psB", [128, 2048], F32))

    qh = (bigA[:, 0, :], bigA[:, 1, :])
    kh = (bigA[:, 2, :], bigA[:, 3, :])

    def pT_slot(j):
        return bigA[:, 4 + j // 4, (j % 4) * 512:(j % 4) * 512 + 512]

    def pT_slot2(j):
        return bigA[:, 4 + j // 4, (j % 4) * 512:(j % 4) * 512 + 1024]
    wlg = (bigB[:, :, 0:256], bigB[:, :, 256:512])
    wlu = (bigB[:, :, 512:768], bigB[:, :, 768:1024])
    wdl = (bigA[:, 0:8, :], bigA[:, 8:16, :])
    gl = (bigB[:, 0:8, 0:512], bigB[:, 0:8, 512:1024])

    def tsc(out_ap, in_ap, s1, s2, op0, op1):
        return lambda e: e.tensor_scalar(out_ap, in_ap, s1, s2, op0, op1)

    def ttt(out_ap, a0, a1, alu):
        return lambda e: e.tensor_tensor(out_ap, a0, a1, alu)

    def act(out_ap, in_ap, func, bias=0.0, scale=1.0, accum_out=None):
        return lambda e: e.activation(out_ap, in_ap, func, bias=bias, scale=scale,
                                      accum_out=accum_out)

    def mmf(out_ap, lhsT, rhs, start, stop):
        return lambda e: e.matmul(out_ap, lhsT, rhs, start=start, stop=stop)

    for dst, src_ in ((g16s, g16), (b16s, b16), (bq16s, bq16), (bk16s, bk16),
                      (bv16s, bv16), (ones_tile, ones_tile_in),
                      (ones_col, ones_col_in), (ones_row, ones_row_in)):
        b.dma("sync", dst[:], src_, sem="cst")
    b.dma("sync", masks.ap(), masks_in.rearrange("(r p) q -> p r q", p=128),
          sem="cst")
    b.dma("sync", fconst[:], fconst_in, sem="cst")
    bo_mis = b.dma("gpsimd", bo_row_s[:], bo_row, sem="mis")
    consts_sq = b.cnt["cst"]
    pe_bo = None
    for n in range(4):
        pe_bo = b.t(mmf(psA[:, n * 512:(n + 1) * 512], ones_tile[0:1, :],
                        bo_row_s[:, n * 512:(n + 1) * 512], True, True),
                    waits=[("cst", consts_sq), ("mis", bo_mis)])
    ve_bo = b.v(lambda e: e.tensor_copy(bo_tile[:], psA[:]), waits=[("pe", pe_bo)])
    psA_free = ("ve", ve_bo)
    psB_free = ("ve", 0)

    prep_ops = []
    cast_done = {}

    def plan_prep():
        for name, src, dst, nrow in (("wg", wg, wg_bf, FF), ("wu", wu, wu_bf, FF),
                                     ("wd", wd, wd_bf, D)):
            nch = 32
            rc = nrow // nch
            for i in range(nch):
                prep_ops.append(("cast", name, dst[i * rc:(i + 1) * rc, :],
                                 src[i * rc:(i + 1) * rc, :]))
        for name, srcb, dstT in (("wg", wg_bf, wgT_h), ("wu", wu_bf, wuT_h)):
            for c in range(NCD):
                for hh in range(4):
                    prep_ops.append(("tr", name, 2048,
                                     srcb[hh * 2048:(hh + 1) * 2048,
                                          c * 128:(c + 1) * 128],
                                     dstT[c * 128:(c + 1) * 128,
                                          hh * 2048:(hh + 1) * 2048]))
        for hc in range(FF // 128):
            prep_ops.append(("tr", "wd", 2048,
                             wd_bf[:, hc * 128:(hc + 1) * 128],
                             wdT_h[hc * 128:(hc + 1) * 128, :]))

    plan_prep()
    prep_i = [0]
    prep_tr_i = [0]
    prep_wb_last = [0, 0]

    def pump_prep(n):
        for _ in range(n):
            i = prep_i[0]
            if i >= len(prep_ops):
                return
            prep_i[0] += 1
            rec = prep_ops[i]
            if rec[0] == "cast":
                _, name, dst, src = rec
                cast_done[name] = b.dma("gpsimd", dst, src, sem="c" + name)
            else:
                _, name, rowsn, src, dst = rec
                half = prep_tr_i[0] % 2
                prep_tr_i[0] += 1
                pbt = (pb0, pb1)[half]
                tcum = b.dma("sync", pbt[:, 0:rowsn], src,
                             waits=[("c" + name, cast_done[name]),
                                    ("aq", prep_wb_last[half])], transpose=True)
                prep_wb_last[half] = b.dma("scalar", dst, pbt[:, 0:rowsn],
                                           waits=[("sq", tcum)])

    ln_idx = [0]

    def ln_tile(x_ap, out_ap, fused_ln1, src_waits):
        k = ln_idx[0] % 4
        ln_idx[0] += 1
        s1 = stats[:, 4 * k:4 * k + 1]
        vv = stats[:, 4 * k + 1:4 * k + 2]
        mean = stats[:, 4 * k + 2:4 * k + 3]
        rstd = stats[:, 4 * k + 3:4 * k + 4]
        c_sum = b.v(lambda e: e.reduce_sum(s1, x_ap, axis=AX.X), waits=src_waits)
        c_sq = b.s(act(sqscr[:], x_ap, AF.Square, accum_out=vv), waits=src_waits)
        c1 = b.v(tsc(mean, s1, 1.0 / D, None, ALU.mult, ALU.bypass),
                 waits=[("ve", c_sum)])
        c2 = b.v(tsc(vv, vv, 1.0 / D, None, ALU.mult, ALU.bypass),
                 waits=[("se", c_sq)])
        msq = stats[:, 16:17]
        c3 = b.v(ttt(msq, mean, mean, ALU.mult), waits=[("ve", c1), ("ve", c2)])
        c4 = b.v(ttt(vv, vv, msq, ALU.subtract), waits=[("ve", c3)])
        if fused_ln1:
            lnscale, lnbias = (1.0 + EPS), fconst[:, 0:1]
        else:
            lnscale, lnbias = 1.0, fconst[:, 1:2]
        c5 = b.s(act(rstd, vv, AF.Ln, bias=lnbias, scale=lnscale),
                 waits=[("ve", c4), ("cst", consts_sq)])
        c6 = b.s(act(rstd, rstd, AF.Exp, scale=-0.5), waits=[("se", c5)])
        c7 = b.v(lambda e: e.scalar_tensor_tensor(mean, mean, -1.0, rstd,
                                                  ALU.mult, ALU.mult),
                 waits=[("se", c6)])
        c8 = b.v(tsc(out_ap, x_ap, rstd, mean, ALU.mult, ALU.add),
                 waits=[("ve", c7)])
        return c8

    x_free = [None, None]
    tfree = {"t0": None, "t1": None, "t2": None}
    bigA_gate = [None]
    bigB_gate = [None]
    wbfree = [None, None]
    sq_xn2_done = [0]
    pe_attn_last = [0]
    maxmm_last = [0]
    maxred_last = [None]

    for s in range(2):
        srow = s * S
        for i in range(16):
            par = i % 2
            xt = (xa, xb)[par]
            nt = (t0, t1)[par]
            ntn = ("t0", "t1")[par]
            ld = b.dma("sync", xt[:], x_in[srow + i * 128:srow + (i + 1) * 128, :],
                       waits=[x_free[par]])
            va = ln_tile(xt[:], nt[:], True, [("sq", ld), tfree[ntn]])
            x_free[par] = ("ve", va)
            wr = b.dma("gpsimd", xn_h[srow + i * 128:srow + (i + 1) * 128, :],
                       nt[:], waits=[("ve", va)], sem=("xn0", "xn1")[par])
            tfree[ntn] = (("xn0", "xn1")[par], wr)
            pump_prep(1)
        p1_xn = [b.cnt["xn0"], b.cnt["xn1"]]

        ve_xnT = 0
        for c in range(NCD):
            w = [("xn0", p1_xn[0]), ("xn1", p1_xn[1])]
            if c == 0 and bigA_gate[0] is not None:
                w.append(bigA_gate[0])
            tr = b.dma("sync", bigA[:, c, :],
                       xn_h[srow:srow + S, c * 128:(c + 1) * 128],
                       waits=w, transpose=True)
            ve_xnT = b.v(tsc(bigA[:, c, :], bigA[:, c, :],
                             g16s[:, c:c + 1], b16s[:, c:c + 1],
                             ALU.mult, ALU.add),
                         waits=[("sq", tr), ("cst", consts_sq)])
            pump_prep(1)

        if xnT_dbg is not None:
            for c in range(NCD):
                b.dma("gpsimd", xnT_dbg[s, :, c, :], bigA[:, c, :],
                      waits=[("ve", ve_xnT)], sem="mis")
            xnT_dbg_mis = b.cnt["mis"]
        else:
            xnT_dbg_mis = 0

        qkv_iters = []
        for wi, (W, dstT, bias_s) in enumerate(
                ((wq, qT_h, bq16s), (wk, kT_h, bk16s), (wv, vT_h, bv16s))):
            Wr = W.rearrange("(c p) n -> p c n", p=128)
            for c in range(NCD):
                qkv_iters.append((wi, Wr, dstT, bias_s, c))
        ldw_cum = [None, None]

        def p3_ldw(it):
            _, Wr_, _, _, c_ = qkv_iters[it]
            par_ = it % 2
            ldw_cum[par_] = b.dma("gpsimd", (wb0, wb1)[par_].ap(),
                                  Wr_[:, :, c_ * 128:(c_ + 1) * 128],
                                  waits=[wbfree[par_]],
                                  sem=("wb0s", "wb1s")[par_])

        p3_ldw(0)
        psA_half_free = [psA_free, psA_free]
        pending = [None]

        def p3_flush_max():
            if pending[0] is None:
                return
            wi_, c_, et_, etn_, sv0, sv1, dstT_ = pending[0]
            pending[0] = None
            vsq = b.v(lambda e, o=sqscr[:], i=et_[:]: e.scalar_tensor_tensor(
                o, i, 1.0, i, ALU.mult, ALU.mult),
                waits=[("se", sv0), ("se", sv1), ("pe", maxmm_last[0])])
            pemax = None
            for n in range(4):
                w = [("ve", vsq)]
                if n == 0:
                    w.append(("cst", consts_sq))
                    if maxred_last[0] is not None:
                        w.append(maxred_last[0])
                pemax = b.t(mmf(psB[:, n * 512:(n + 1) * 512], ones_tile[:],
                                sqscr[:, n * 512:(n + 1) * 512], True, True),
                            waits=w)
            maxmm_last[0] = pemax
            dcol = hstats[:, wi_ * 16 + c_:wi_ * 16 + c_ + 1]
            vred = b.v(lambda e, o=dcol: e.reduce_max(o, psB[:], axis=AX.X),
                       waits=[("pe", pemax)])
            maxred_last[0] = ("ve", vred)
            esem = "et0" if etn_ == "t0" else "et1"
            wr = b.dma("gpsimd", dstT_[s * D + c_ * 128:s * D + (c_ + 1) * 128, :],
                       et_[:], waits=[("se", sv0), ("se", sv1), ("ve", vsq)],
                       sem=esem)
            tfree[etn_] = (esem, wr)

        for it, (wi, Wr, dstT, bias_s, c) in enumerate(qkv_iters):
            par = it % 2
            wbt = (wb0, wb1)[par]
            ldw = ldw_cum[par]
            if it + 1 < len(qkv_iters):
                p3_ldw(it + 1)
            et = (t0, t1)[par]
            etn = ("t0", "t1")[par]
            sevs = []
            pe_last = None
            for half in range(2):
                h0 = half * 1024
                for dc in range(NCD):
                    for n in range(2):
                        w = []
                        if dc == 0 and n == 0:
                            w = [(("wb0s", "wb1s")[par], ldw), ("ve", ve_xnT),
                                 psA_half_free[half]]
                        pe_last = b.t(mmf(psA[:, h0 + n * 512:h0 + (n + 1) * 512],
                                          wbt[:, dc, :],
                                          bigA[:, dc, h0 + n * 512:h0 + (n + 1) * 512],
                                          dc == 0, dc == NCD - 1), waits=w)
                sev = b.s(act(et[:, h0:h0 + 1024], psA[:, h0:h0 + 1024],
                              AF.Identity, bias=bias_s[:, c:c + 1]),
                          waits=[("pe", pe_last), tfree[etn],
                                 ("cst", consts_sq)])
                psA_half_free[half] = ("se", sev)
                sevs.append(sev)
                if half == 0:
                    p3_flush_max()
            wbfree[par] = ("pe", pe_last)
            if wi < 2:
                pending[0] = (wi, c, et, etn, sevs[0], sevs[1], dstT)
            else:
                esem = ("et0", "et1")[par]
                wr = b.dma("gpsimd", dstT[s * D + c * 128:s * D + (c + 1) * 128, :],
                           et[:], waits=[("se", sevs[0]), ("se", sevs[1])],
                           sem=esem)
                tfree[etn] = (esem, wr)
            pump_prep(1)
        p3_flush_max()
        q2 = hstats[:, 0:16]
        k2 = hstats[:, 16:32]
        negc_all = hstats[:, 32:48]
        sl1 = b.s(act(q2, q2, AF.Ln, bias=fconst[:, 2:3]),
                  waits=[maxred_last[0], ("cst", consts_sq)])
        sl2 = b.s(act(k2, k2, AF.Ln, bias=fconst[:, 2:3]), waits=[("se", sl1)])
        v3 = b.v(ttt(negc_all, q2, k2, ALU.add), waits=[("se", sl2)])
        s4 = b.s(act(negc_all, negc_all, AF.Exp, scale=0.5, bias=fconst[:, 3:4]),
                 waits=[("ve", v3)])
        negc_done = b.s(act(negc_all, negc_all, AF.Identity, scale=-1.0),
                        waits=[("se", s4)])
        p3_pe = b.cnt["pe"]
        p3_et = [b.cnt["et0"], b.cnt["et1"]]

        for tt in range(NCD):
            w = [("et0", p3_et[0]), ("et1", p3_et[1])]
            if tt == 0 and bigB_gate[0] is not None:
                w.append(bigB_gate[0])
            b.dma("sync", bigB[:, tt, :],
                  vT_h[s * D:s * D + D, tt * 128:(tt + 1) * 128],
                  waits=w, transpose=True)
        vseq_sq = b.cnt["sq"]

        qh_free = [("pe", p3_pe), ("pe", p3_pe)]
        kh_free = [("pe", p3_pe), ("pe", p3_pe)]
        slot_free = [None] * 16
        sfree = [psA_half_free[0], psA_half_free[0],
                 psA_half_free[1], psA_half_free[1]]
        ofree = [maxred_last[0], maxred_last[0]]
        sumfree = [maxred_last[0], maxred_last[0]]
        rtfree = [None, None]
        evfree = [None, None]
        for h in range(NH):
            par = h % 2
            qt = qh[par]
            kt = kh[par]
            ldq = b.dma("sync", qt, qT_h[s * D + h * 128:s * D + (h + 1) * 128, :],
                        waits=[qh_free[par], ("sq", vseq_sq if h == 0 else 0),
                               ("et0", p3_et[0]) if h < 2 else None,
                               ("et1", p3_et[1]) if h < 2 else None,
                               ("mis", xnT_dbg_mis) if h < 2 else None])
            ldk = b.dma("sync", kt, kT_h[s * D + h * 128:s * D + (h + 1) * 128, :],
                        waits=[kh_free[par]])
            negc = negc_all[:, h:h + 1]
            pe_S_last = 0
            jobs = []
            for Q in range(4):
                for j in range(4 * Q + 4):
                    jobs.append((Q, j))
            pdeps = {}
            po = {"pe_O": None}

            def emit_avsum(idx):
                Q, j = jobs[idx]
                qpar = Q % 2
                nj = 4 * Q + 4
                o_ps = psB[:, qpar * 512:(qpar + 1) * 512]
                sum_ps = psB[:, (2 + qpar) * 512:(3 + qpar) * 512]
                pdep = pdeps.pop(idx)
                w = [pdep]
                if j == 0:
                    w += [ofree[qpar], ("sq", vseq_sq)]
                pe_O = b.t(mmf(o_ps, bigB[:, j, h * 128:(h + 1) * 128],
                               pT_slot(j), j == 0, j == nj - 1), waits=w)
                w = [pdep] + ([sumfree[qpar], ("cst", consts_sq)]
                              if j == 0 else [])
                pe_sum = b.t(mmf(sum_ps, ones_tile[:], pT_slot(j),
                                 j == 0, j == nj - 1), waits=w)
                slot_free[j] = ("pe", pe_sum)
                if j == nj - 1:
                    rtt = (rt0, rt1)[qpar]
                    ln_s = b.s(act(rtt[:], sum_ps, AF.Ln),
                               waits=[("pe", pe_sum), rtfree[qpar]])
                    sumfree[qpar] = ("se", ln_s)
                    ex_s = b.s(act(rtt[:], rtt[:], AF.Exp, scale=-1.0),
                               waits=[("se", ln_s)])
                    evt = (ev0, ev1)[qpar]
                    vev = b.v(ttt(evt[:], o_ps, rtt[:], ALU.mult),
                              waits=[("se", ex_s), ("pe", pe_O), evfree[qpar]])
                    ofree[qpar] = ("ve", vev)
                    rtfree[qpar] = ("ve", vev)
                    wr = b.dma("gpsimd",
                               avT_h[s * D + h * 128:s * D + (h + 1) * 128,
                                     Q * 512:(Q + 1) * 512], evt[:],
                               waits=[("ve", vev)], sem=("av0", "av1")[qpar])
                    evfree[qpar] = (("av0", "av1")[qpar], wr)

            for idx, (Q, j) in enumerate(jobs):
                bank = idx % 4
                s_ps1 = psA[:, bank * 512:(bank + 1) * 512]
                w = [sfree[bank]]
                if idx == 0:
                    w += [("sq", ldq), ("sq", ldk)]
                mmS = b.t(mmf(s_ps1, kt[:, j * 128:(j + 1) * 128],
                              qt[:, Q * 512:(Q + 1) * 512], True, True),
                          waits=w)
                pe_S_last = mmS
                ex = b.s(act(pT_slot(j), s_ps1, AF.Exp, bias=negc,
                             scale=INV_SQRT_DH),
                         waits=[("pe", mmS), slot_free[j],
                                ("se", negc_done) if idx == 0 and h == 0
                                else None])
                sfree[bank] = ("se", ex)
                pdep = ("se", ex)
                if j >= 4 * Q:
                    r = j - 4 * Q
                    vm = b.v(ttt(pT_slot(j), pT_slot(j),
                                 masks[:, r, :], ALU.mult),
                             waits=[("se", ex), ("cst", consts_sq)])
                    pdep = ("ve", vm)
                pdeps[idx] = pdep
                if idx >= 2:
                    emit_avsum(idx - 2)
            emit_avsum(len(jobs) - 2)
            emit_avsum(len(jobs) - 1)
            qh_free[par] = ("pe", pe_S_last)
            kh_free[par] = ("pe", pe_S_last)
            pump_prep(2)
        p4_av = [b.cnt["av0"], b.cnt["av1"]]
        pe_attn_last[0] = b.cnt["pe"]

        for din in range(NCD):
            w = [("av0", p4_av[0]), ("av1", p4_av[1])]
            if din == 0:
                w.append(("pe", pe_attn_last[0]))
            b.dma("sync", bigA[:, din, :],
                  avT_h[s * D + din * 128:s * D + (din + 1) * 128, :], waits=w)
        avT_sq = b.cnt["sq"]
        Wor = wo.rearrange("(c p) n -> p c n", p=128)

        def p5_ldw(c_):
            par_ = c_ % 2
            ldw_cum[par_] = b.dma("gpsimd", (wb0, wb1)[par_].ap(),
                                  Wor[:, :, c_ * 128:(c_ + 1) * 128],
                                  waits=[wbfree[par_]],
                                  sem=("wb0s", "wb1s")[par_])

        p5_ldw(0)
        for c in range(NCD):
            par = c % 2
            wbt = (wb0, wb1)[par]
            ldw = ldw_cum[par]
            if c + 1 < NCD:
                p5_ldw(c + 1)
            ps = (psA, psB)[par]
            ps_gate = psA_free if par == 0 else psB_free
            if c < 2:
                ps_gate_extra = [sfree[0], sfree[1], sfree[2], sfree[3],
                                 ofree[0], ofree[1], sumfree[0], sumfree[1]]
            else:
                ps_gate_extra = []
            pe_last = None
            for dc in range(NCD):
                for n in range(4):
                    w = []
                    if dc == 0 and n == 0:
                        w = [(("wb0s", "wb1s")[par], ldw), ("sq", avT_sq),
                             ps_gate] + ps_gate_extra
                    pe_last = b.t(mmf(ps[:, n * 512:(n + 1) * 512], wbt[:, dc, :],
                                      bigA[:, dc, n * 512:(n + 1) * 512],
                                      dc == 0, dc == NCD - 1), waits=w)
            wbfree[par] = ("pe", pe_last)
            et = (t0, t1)[par]
            etn = ("t0", "t1")[par]
            sev = b.s(act(et[:], ps[:], AF.Identity), waits=[("pe", pe_last), tfree[etn]])
            if par == 0:
                psA_free = ("se", sev)
            else:
                psB_free = ("se", sev)
            wr = b.dma("gpsimd", attnT_h[s * D + c * 128:s * D + (c + 1) * 128, :],
                       et[:], waits=[("se", sev)], sem=("et0", "et1")[par])
            tfree[etn] = (("et0", "et1")[par], wr)
            pump_prep(2)
        p5a_et = [b.cnt["et0"], b.cnt["et1"]]
        p5a_pe = b.cnt["pe"]
        bigA_gate[0] = ("pe", p5a_pe)
        bigB_gate[0] = ("pe", pe_attn_last[0])

        t2_free = tfree["t2"]
        for i in range(16):
            par = i % 2
            zt = (t0, t1)[par]
            ztn = ("t0", "t1")[par]
            tr = b.dma("sync", zt[:],
                       attnT_h[s * D:s * D + D, i * 128:(i + 1) * 128],
                       waits=[("et0", p5a_et[0]), ("et1", p5a_et[1]),
                              tfree[ztn]], transpose=True)
            ldx = b.dma("sync", (xa, xb)[par][:],
                        x_in[srow + i * 128:srow + (i + 1) * 128, :],
                        waits=[x_free[par]])
            v1 = b.v(ttt(zt[:], zt[:], bo_tile[:], ALU.add),
                     waits=[("sq", tr), ("ve", ve_bo)])
            va = ln_tile(zt[:], f0[:], False, [("ve", v1), tfree["f0"] if "f0" in tfree else None])
            tfree[ztn] = ("ve", va)
            v2 = b.v(ttt(f0[:], f0[:], (xa, xb)[par][:], ALU.add),
                     waits=[("ve", va), ("sq", ldx)])
            x_free[par] = ("ve", v2)
            wr1 = b.dma("gpsimd", x1_h[srow + i * 128:srow + (i + 1) * 128, :],
                        f0[:], waits=[("ve", v2)], sem="xf")
            vb = ln_tile(f0[:], t2[:], False, [("ve", v2), t2_free])
            wr2 = b.dma("gpsimd", xn2_h[srow + i * 128:srow + (i + 1) * 128, :],
                        t2[:], waits=[("ve", vb)], sem="xf")
            tfree["f0"] = ("xf", wr2)
            t2_free = ("xf", wr2)
            pump_prep(1)
        tfree["t2"] = t2_free
        sq_xn2_done[0] = b.cnt["xf"]

    pump_prep(len(prep_ops))
    prep_aq = b.cnt["aq"]
    silu_t = sqscr
    silu_free = [0]
    wl_free = [None, None]
    gated_free = [tfree["t0"], tfree["t1"]]
    ve_gate_last = [0]
    pe_p6_first_gate = [bigA_gate[0]]
    p6_bigA_sq = [0]
    for H in range(2):
        trow = H * 2048
        tr_last = 0
        for c in range(NCD):
            w = [("xf", sq_xn2_done[0])]
            if c == 0:
                if H == 0:
                    w += pe_p6_first_gate
                else:
                    w += [("pe", b.cnt["pe"])]
            tr_last = b.dma("sync", bigA[:, c, :],
                            xn2_h[trow:trow + 2048, c * 128:(c + 1) * 128],
                            waits=w, transpose=True)
        WgTr = wgT_h.rearrange("(c p) h -> p c h", p=128)
        WuTr = wuT_h.rearrange("(c p) h -> p c h", p=128)
        for hp in range(32):
            par = hp % 2
            w = [("aq", prep_aq), wl_free[par]]
            if hp == 0 and H == 0:
                w.append(bigB_gate[0])
            ldg = b.dma("sync", wlg[par],
                        WgTr[:, :, hp * 256:(hp + 1) * 256], waits=w)
            ldu = b.dma("sync", wlu[par],
                        WuTr[:, :, hp * 256:(hp + 1) * 256])
            pe_wl_last = None
            for half in range(2):
                h = 2 * hp + half
                hs = half * 128
                peg = None
                for dc in range(NCD):
                    for n in range(4):
                        w = []
                        if dc == 0 and n == 0:
                            w = [("sq", ldu if half == 0 else 0),
                                 ("sq", tr_last), psA_free]
                        peg = b.t(mmf(psA[:, n * 512:(n + 1) * 512],
                                      wlg[par][:, dc, hs:hs + 128],
                                      bigA[:, dc, n * 512:(n + 1) * 512],
                                      dc == 0, dc == NCD - 1), waits=w)
                sg = b.s(act(silu_t[:], psA[:], AF.Silu),
                         waits=[("pe", peg), ("ve", silu_free[0])])
                psA_free = ("se", sg)
                peu = None
                for dc in range(NCD):
                    for n in range(4):
                        w = [psB_free] if (dc == 0 and n == 0) else []
                        peu = b.t(mmf(psB[:, n * 512:(n + 1) * 512],
                                      wlu[par][:, dc, hs:hs + 128],
                                      bigA[:, dc, n * 512:(n + 1) * 512],
                                      dc == 0, dc == NCD - 1), waits=w)
                pe_wl_last = peu
                gt = (t0, t1)[half]
                vg = b.v(ttt(gt[:], psB[:], silu_t[:], ALU.mult),
                         waits=[("pe", peu), ("se", sg), gated_free[half]])
                silu_free[0] = vg
                psB_free = ("ve", vg)
                wr = b.dma("gpsimd", gatedT_h[h * 128:(h + 1) * 128,
                                              trow:trow + 2048], gt[:],
                           waits=[("ve", vg)], sem=("gt0", "gt1")[half])
                gated_free[half] = (("gt0", "gt1")[half], wr)
            wl_free[par] = ("pe", pe_wl_last)
        if H == 0:
            p6_gt_h0 = [b.cnt["gt0"], b.cnt["gt1"]]
    p6_gt = [b.cnt["gt0"], b.cnt["gt1"]]
    p6_pe = b.cnt["pe"]

    stack2.close()
    accum = sb("accum", [128, 4, 2048], F32)
    WdF = wdT_h.rearrange("(F p) d -> F p d", p=128)
    GTr = gatedT_h.rearrange("(G hc p) t -> G p hc t", G=8, hc=8, p=128)
    wd_slot_free = [("pe", p6_pe)] * 16
    wd_slot_cum = [0] * 16
    gl_bufs = [bigB[:, 0:8, 1024:1536], bigB[:, 8:16, 1024:1536],
               bigB[:, 0:8, 1536:2048], bigB[:, 8:16, 1536:2048]]
    gl_free = [bigB_gate[0]] * 4
    gl_cum = [0] * 4
    blocks = [(e, gi) for e in range(8) for gi in range(8)]

    def p7_load_block(bidx):
        if bidx >= len(blocks):
            return
        e_, gi_ = blocks[bidx]
        bi = bidx % 4
        gl_cum[bi] = b.dma("sync", gl_bufs[bi],
                           GTr[gi_][:, :, e_ * 512:e_ * 512 + 512],
                           waits=[gl_free[bi], ("gt0", p6_gt[0]),
                                  ("gt1", p6_gt[1])])
        for hc in range(8):
            sl = (gi_ * 8 + hc) % 16
            wd_slot_cum[sl] = b.dma("scalar", bigA[:, sl, :], WdF[gi_ * 8 + hc],
                                    waits=[wd_slot_free[sl], ("aq", prep_aq)])

    p7_load_block(0)
    p7_load_block(1)
    acc_free = [None, None, None, None]
    f0_free = tfree["f0"]
    out_sq_last = 0
    for bidx, (e, gi) in enumerate(blocks):
        glb = gl_bufs[bidx % 4]
        pe_by_hc = [None] * 8
        pe_g_last = None
        for t in range(4):
            ps = (psA, psB)[t % 2]
            ps_gate = psA_free if t % 2 == 0 else psB_free
            for hc in range(8):
                sl = (gi * 8 + hc) % 16
                for n in range(4):
                    w = []
                    if n == 0:
                        w.append(("aq", wd_slot_cum[sl]))
                        if hc == 0:
                            w += [("sq", gl_cum[bidx % 4]), ps_gate]
                            if gi == 0:
                                w.append(acc_free[t])
                    pet = b.t(mmf(ps[:, n * 512:(n + 1) * 512],
                                  glb[:, hc, t * 128:(t + 1) * 128],
                                  bigA[:, sl, n * 512:(n + 1) * 512],
                                  hc == 0, hc == 7), waits=w)
                pe_by_hc[hc] = pet
            pe_g_last = pet
            if gi == 0:
                vacc = b.v(lambda e_, o=accum[:, t, :], i=ps[:]: e_.tensor_copy(o, i),
                           waits=[("pe", pet)])
            else:
                vacc = b.v(ttt(accum[:, t, :], accum[:, t, :], ps[:], ALU.add),
                           waits=[("pe", pet)])
            if t % 2 == 0:
                psA_free = ("ve", vacc)
            else:
                psB_free = ("ve", vacc)
            acc_free[t] = ("ve", vacc)
            if gi == 7:
                par = t % 2
                row0 = e * 512 + t * 128
                ldx1 = b.dma("sync", (xa, xb)[par][:], x1_h[row0:row0 + 128, :],
                             waits=[x_free[par], ("xf", sq_xn2_done[0])])
                va = ln_tile(accum[:, t, :], f0[:], False, [acc_free[t], f0_free])
                v2 = b.v(ttt(f0[:], f0[:], (xa, xb)[par][:], ALU.add),
                         waits=[("ve", va), ("sq", ldx1)])
                x_free[par] = ("ve", v2)
                acc_free[t] = ("ve", va)
                wro = b.dma("gpsimd", out[row0:row0 + 128, :], f0[:],
                            waits=[("ve", v2)], sem="outs")
                f0_free = ("outs", wro)
                out_sq_last = wro
        for hc in range(8):
            wd_slot_free[(gi * 8 + hc) % 16] = ("pe", pe_by_hc[hc])
        gl_free[bidx % 4] = ("pe", pe_g_last)
        p7_load_block(bidx + 2)
    b.op("gpsimd", lambda e: e.nop(), waits=[("outs", out_sq_last)])

    b.emit()
    stack.close()
    return nc


_CACHE = {}


def _get_nc():
    if "nc" not in _CACHE:
        _CACHE["nc"] = build_program()
    return _CACHE["nc"]


def kernel(L0_attn_X, L0_attn_ln_g, L0_attn_ln_b, L0_attn_Wq, L0_attn_bq,
           L0_attn_Wk, L0_attn_bk, L0_attn_Wv, L0_attn_bv, L0_attn_Wo,
           L0_attn_bo, L0_ffn_w_gate, L0_ffn_w_up, L0_ffn_w_down):
    X = np.ascontiguousarray(np.asarray(L0_attn_X, dtype=np.float32))
    B = X.shape[0]
    n_cores = 8
    per = B // n_cores

    def cols16(v):
        return np.ascontiguousarray(np.asarray(v, np.float32).reshape(NCD, 128).T)

    masks = np.zeros((4, 128, 512), np.float32)
    for r in range(4):
        for k in range(128):
            masks[r, k, 128 * r + k:] = 1.0
    masks_bf = masks.reshape(512, 512).astype(ml_dtypes.bfloat16)

    common = {
        "wq": np.ascontiguousarray(np.asarray(L0_attn_Wq, np.float32)),
        "wk": np.ascontiguousarray(np.asarray(L0_attn_Wk, np.float32)),
        "wv": np.ascontiguousarray(np.asarray(L0_attn_Wv, np.float32)),
        "wo": np.ascontiguousarray(np.asarray(L0_attn_Wo, np.float32)),
        "wg": np.ascontiguousarray(np.asarray(L0_ffn_w_gate, np.float32)),
        "wu": np.ascontiguousarray(np.asarray(L0_ffn_w_up, np.float32)),
        "wd": np.ascontiguousarray(np.asarray(L0_ffn_w_down, np.float32)),
        "g16": cols16(L0_attn_ln_g),
        "b16": cols16(L0_attn_ln_b),
        "bq16": cols16(L0_attn_bq),
        "bk16": cols16(L0_attn_bk),
        "bv16": cols16(L0_attn_bv),
        "bo_row": np.ascontiguousarray(np.asarray(L0_attn_bo, np.float32)[None, :]),
        "ones_tile": np.ones((128, 128), ml_dtypes.bfloat16),
        "ones_col": np.ones((128, 1), ml_dtypes.bfloat16),
        "ones_row_f32": np.ones((1, 128), np.float32),
        "masks": masks_bf,
        "fconst": np.tile(np.array([[EPS * EPS, EPS, 1e-30, -HALF_LN_DH]],
                                   np.float32), (128, 1)),
    }
    in_maps = []
    for c in range(n_cores):
        m = dict(common)
        m["x_in"] = np.ascontiguousarray(X[c * per:(c + 1) * per].reshape(T, D))
        in_maps.append(m)

    nc = _get_nc()
    res = run_bass_kernel_spmd(nc, in_maps, core_ids=list(range(n_cores)))
    outs = [res.results[c]["out"].reshape(per, S, D) for c in range(n_cores)]
    return np.concatenate(outs, axis=0)


if __name__ == "__main__":
    np.random.seed(0)
    nc = build_program()
    print("build ok")


# version 14 (final)
# speedup vs baseline: 1.0372x; 1.0372x over previous
import math
import contextlib
import numpy as np
import ml_dtypes
import concourse.bass as bass
import concourse.mybir as mybir
from concourse.bass_utils import run_bass_kernel_spmd

F32 = mybir.dt.float32
BF16 = mybir.dt.bfloat16
AF = mybir.ActivationFunctionType
ALU = mybir.AluOpType
AX = mybir.AxisListType

D = 2048
S = 2048
T = 4096
FF = 8192
NH = 16
DH = 128
NCD = 16
EPS = 1e-5
INV_SQRT_DH = 1.0 / math.sqrt(DH)
HALF_LN_DH = 0.5 * math.log(DH)


class Builder:

    def __init__(self, nc):
        self.nc = nc
        self.ops = {"sync": [], "tensor": [], "vector": [], "scalar": [], "gpsimd": []}
        self.cnt = {}
        self.sem = {}

    def make_sems(self, names, stack):
        for n in names:
            self.sem[n] = stack.enter_context(self.nc.semaphore("sem_" + n))
            self.cnt[n] = 0

    def op(self, engine, fn, waits=(), inc=None, amt=1):
        waits = [w for w in waits if w is not None]
        if inc is not None:
            self.cnt[inc] += amt
        self.ops[engine].append((tuple(waits), fn, inc, amt))
        return self.cnt[inc] if inc is not None else None

    def t(self, fn, waits=()):
        return self.op("tensor", fn, waits, inc="pe")

    def v(self, fn, waits=()):
        return self.op("vector", fn, waits, inc="ve")

    def s(self, fn, waits=()):
        return self.op("scalar", fn, waits, inc="se")

    def dma(self, queue, out_ap, in_ap, waits=(), transpose=False, sem=None):
        qsem = sem or {"sync": "sq", "scalar": "aq", "gpsimd": "pq"}[queue]
        if transpose:
            fn = lambda e, o=out_ap, i=in_ap: e.dma_start(o, i, transpose=True)
        else:
            fn = lambda e, o=out_ap, i=in_ap: e.dma_start(o, i)
        return self.op(queue, fn, waits=waits, inc=qsem, amt=16)

    def emit(self):
        b = self

        def mk(engine):
            def run(eng):
                for waits, fn, inc, amt in b.ops[engine]:
                    for (sname, val) in waits:
                        if val > 0:
                            eng.wait_ge(b.sem[sname], val)
                    inst = fn(eng)
                    if inc is not None:
                        inst.then_inc(b.sem[inc], amt)
            return run

        with self.nc.Block() as block:
            block.sync(mk("sync"))
            block.tensor(mk("tensor"))
            block.vector(mk("vector"))
            block.scalar(mk("scalar"))
            block.gpsimd(mk("gpsimd"))


def build_program(debug=()):
    nc = bass.Bass("TRN2", target_bir_lowering=False)

    x_in = nc.dram_tensor("x_in", [T, D], F32, kind="ExternalInput").ap()
    wq = nc.dram_tensor("wq", [D, D], F32, kind="ExternalInput").ap()
    wk = nc.dram_tensor("wk", [D, D], F32, kind="ExternalInput").ap()
    wv = nc.dram_tensor("wv", [D, D], F32, kind="ExternalInput").ap()
    wo = nc.dram_tensor("wo", [D, D], F32, kind="ExternalInput").ap()
    wg = nc.dram_tensor("wg", [FF, D], F32, kind="ExternalInput").ap()
    wu = nc.dram_tensor("wu", [FF, D], F32, kind="ExternalInput").ap()
    wd = nc.dram_tensor("wd", [D, FF], F32, kind="ExternalInput").ap()
    g16 = nc.dram_tensor("g16", [128, NCD], F32, kind="ExternalInput").ap()
    b16 = nc.dram_tensor("b16", [128, NCD], F32, kind="ExternalInput").ap()
    bq16 = nc.dram_tensor("bq16", [128, NCD], F32, kind="ExternalInput").ap()
    bk16 = nc.dram_tensor("bk16", [128, NCD], F32, kind="ExternalInput").ap()
    bv16 = nc.dram_tensor("bv16", [128, NCD], F32, kind="ExternalInput").ap()
    bo_row = nc.dram_tensor("bo_row", [1, D], F32, kind="ExternalInput").ap()
    ones_tile_in = nc.dram_tensor("ones_tile", [128, 128], BF16, kind="ExternalInput").ap()
    ones_col_in = nc.dram_tensor("ones_col", [128, 1], BF16, kind="ExternalInput").ap()
    ones_row_in = nc.dram_tensor("ones_row_f32", [1, 128], F32, kind="ExternalInput").ap()
    masks_in = nc.dram_tensor("masks", [512, 512], BF16, kind="ExternalInput").ap()
    fconst_in = nc.dram_tensor("fconst", [128, 4], F32, kind="ExternalInput").ap()

    out = nc.dram_tensor("out", [T, D], F32, kind="ExternalOutput").ap()

    def kindof(name):
        return "ExternalOutput" if name in debug else "Internal"
    xn_h = nc.dram_tensor("xn_h", [T, D], BF16, kind=kindof("xn_h")).ap()
    qT_h = nc.dram_tensor("qT_h", [2 * D, S], BF16, kind=kindof("qT_h")).ap()
    kT_h = nc.dram_tensor("kT_h", [2 * D, S], BF16, kind=kindof("kT_h")).ap()
    vT_h = nc.dram_tensor("vT_h", [2 * D, S], BF16, kind=kindof("vT_h")).ap()
    avT_h = nc.dram_tensor("avT_h", [2 * D, S], BF16, kind=kindof("avT_h")).ap()
    attnT_h = nc.dram_tensor("attnT_h", [2 * D, S], BF16, kind=kindof("attnT_h")).ap()
    x1_h = nc.dram_tensor("x1_h", [T, D], F32, kind=kindof("x1_h")).ap()
    xn2_h = nc.dram_tensor("xn2_h", [T, D], BF16, kind=kindof("xn2_h")).ap()
    wg_bf = nc.dram_tensor("wg_bf", [FF, D], BF16, kind=kindof("wg_bf")).ap()
    wu_bf = nc.dram_tensor("wu_bf", [FF, D], BF16, kind=kindof("wu_bf")).ap()
    wd_bf = nc.dram_tensor("wd_bf", [D, FF], BF16, kind=kindof("wd_bf")).ap()
    wgT_h = nc.dram_tensor("wgT_h", [D, FF], BF16, kind=kindof("wgT_h")).ap()
    xnT_dbg = (nc.dram_tensor("xnT_dbg", [2, 128, NCD, 2048], BF16,
                              kind="ExternalOutput").ap()
               if "xnT_dbg" in debug else None)
    wuT_h = nc.dram_tensor("wuT_h", [D, FF], BF16, kind=kindof("wuT_h")).ap()
    wdT_h = nc.dram_tensor("wdT_h", [FF, D], BF16, kind=kindof("wdT_h")).ap()
    gatedT_h = nc.dram_tensor("gatedT_h", [FF, T], BF16, kind=kindof("gatedT_h")).ap()

    stack = contextlib.ExitStack()
    b = Builder(nc)
    b.make_sems(["sq", "aq", "pq", "pe", "ve", "se",
                 "mis", "cst", "cwg", "cwu", "cwd", "wb0s", "wb1s",
                 "et0", "et1", "av0", "av1", "xf",
                 "xn0", "xn1", "gt0", "gt1", "outs",
                 "xnt", "qk0", "qk1", "vsq0", "avl"], stack)

    stack2 = contextlib.ExitStack()

    def sb(name, shape, dtype, stk=None):
        return (stk or stack).enter_context(nc.sbuf_tensor(name, shape, dtype))

    bigA = sb("bigA", [128, NCD, 2048], BF16)
    bigB = sb("bigB", [128, NCD, 2048], BF16)
    stats = sb("stats", [128, 24], F32)
    hstats = sb("hstats", [128, 48], F32)
    xa = sb("xa", [128, D], F32)
    xb = sb("xb", [128, D], F32)
    t0 = sb("t0", [128, D], BF16)
    t1 = sb("t1", [128, D], BF16)
    t2 = sb("t2", [128, D], BF16)
    f0 = sb("f0", [128, D], F32)
    sqscr = sb("sqscr", [128, D], BF16)
    fconst = sb("fconst_s", [128, 4], F32)
    g16s = sb("g16s", [128, NCD], F32, stack2)
    b16s = sb("b16s", [128, NCD], F32, stack2)
    bq16s = sb("bq16s", [128, NCD], F32, stack2)
    bk16s = sb("bk16s", [128, NCD], F32, stack2)
    bv16s = sb("bv16s", [128, NCD], F32, stack2)
    bo_tile = sb("bo_tile", [128, D], BF16, stack2)
    bo_row_s = sb("bo_row_s", [1, D], BF16, stack2)
    ones_tile = sb("ones_tile_s", [128, 128], BF16, stack2)
    ones_col = sb("ones_col_s", [128, 1], BF16, stack2)
    ones_row = sb("ones_row_s", [1, 128], F32, stack2)
    masks = sb("masks_s", [128, 4, 512], BF16, stack2)
    rows = sb("rows", [1, 1024], F32, stack2)
    wb0 = sb("wb0", [128, NCD, 128], BF16, stack2)
    wb1 = sb("wb1", [128, NCD, 128], BF16, stack2)
    rt0 = sb("rt0", [128, 512], F32, stack2)
    rt1 = sb("rt1", [128, 512], F32, stack2)
    pb0 = sb("pb0", [128, 2048], BF16, stack2)
    pb1 = sb("pb1", [128, 2048], BF16, stack2)
    ev0 = sb("ev0", [128, 512], BF16, stack2)
    ev1 = sb("ev1", [128, 512], BF16, stack2)
    accum = None

    psA = stack.enter_context(nc.psum_tensor("psA", [128, 2048], F32))
    psB = stack.enter_context(nc.psum_tensor("psB", [128, 2048], F32))

    qh = (bigA[:, 0, :], bigA[:, 1, :])
    kh = (bigA[:, 2, :], bigA[:, 3, :])

    def pT_slot(j):
        return bigA[:, 4 + j // 4, (j % 4) * 512:(j % 4) * 512 + 512]

    def pT_slot2(j):
        return bigA[:, 4 + j // 4, (j % 4) * 512:(j % 4) * 512 + 1024]
    wlg = (bigB[:, :, 0:256], bigB[:, :, 256:512])
    wlu = (bigB[:, :, 512:768], bigB[:, :, 768:1024])
    wdl = (bigA[:, 0:8, :], bigA[:, 8:16, :])
    gl = (bigB[:, 0:8, 0:512], bigB[:, 0:8, 512:1024])

    def tsc(out_ap, in_ap, s1, s2, op0, op1):
        return lambda e: e.tensor_scalar(out_ap, in_ap, s1, s2, op0, op1)

    def ttt(out_ap, a0, a1, alu):
        return lambda e: e.tensor_tensor(out_ap, a0, a1, alu)

    def act(out_ap, in_ap, func, bias=0.0, scale=1.0, accum_out=None):
        return lambda e: e.activation(out_ap, in_ap, func, bias=bias, scale=scale,
                                      accum_out=accum_out)

    def mmf(out_ap, lhsT, rhs, start, stop):
        return lambda e: e.matmul(out_ap, lhsT, rhs, start=start, stop=stop)

    for dst, src_ in ((g16s, g16), (b16s, b16), (bq16s, bq16), (bk16s, bk16),
                      (bv16s, bv16), (ones_tile, ones_tile_in),
                      (ones_col, ones_col_in), (ones_row, ones_row_in)):
        b.dma("sync", dst[:], src_, sem="cst")
    b.dma("sync", masks.ap(), masks_in.rearrange("(r p) q -> p r q", p=128),
          sem="cst")
    b.dma("sync", fconst[:], fconst_in, sem="cst")
    bo_mis = b.dma("gpsimd", bo_row_s[:], bo_row, sem="mis")
    consts_sq = b.cnt["cst"]
    pe_bo = None
    for n in range(4):
        pe_bo = b.t(mmf(psA[:, n * 512:(n + 1) * 512], ones_tile[0:1, :],
                        bo_row_s[:, n * 512:(n + 1) * 512], True, True),
                    waits=[("cst", consts_sq), ("mis", bo_mis)])
    ve_bo = b.v(lambda e: e.tensor_copy(bo_tile[:], psA[:]), waits=[("pe", pe_bo)])
    psA_free = ("ve", ve_bo)
    psB_free = ("ve", 0)

    prep_ops = []
    cast_done = {}

    def plan_prep():
        for name, src, dst, nrow in (("wg", wg, wg_bf, FF), ("wu", wu, wu_bf, FF),
                                     ("wd", wd, wd_bf, D)):
            nch = 32
            rc = nrow // nch
            for i in range(nch):
                prep_ops.append(("cast", name, dst[i * rc:(i + 1) * rc, :],
                                 src[i * rc:(i + 1) * rc, :]))
        for name, srcb, dstT in (("wg", wg_bf, wgT_h), ("wu", wu_bf, wuT_h)):
            for c in range(NCD):
                for hh in range(4):
                    prep_ops.append(("tr", name, 2048,
                                     srcb[hh * 2048:(hh + 1) * 2048,
                                          c * 128:(c + 1) * 128],
                                     dstT[c * 128:(c + 1) * 128,
                                          hh * 2048:(hh + 1) * 2048]))
        for hc in range(FF // 128):
            prep_ops.append(("tr", "wd", 2048,
                             wd_bf[:, hc * 128:(hc + 1) * 128],
                             wdT_h[hc * 128:(hc + 1) * 128, :]))

    plan_prep()
    prep_i = [0]
    prep_tr_i = [0]
    prep_wb_last = [0, 0]

    def pump_prep(n):
        for _ in range(n):
            i = prep_i[0]
            if i >= len(prep_ops):
                return
            prep_i[0] += 1
            rec = prep_ops[i]
            if rec[0] == "cast":
                _, name, dst, src = rec
                cast_done[name] = b.dma("gpsimd", dst, src, sem="c" + name)
            else:
                _, name, rowsn, src, dst = rec
                half = prep_tr_i[0] % 2
                prep_tr_i[0] += 1
                pbt = (pb0, pb1)[half]
                tcum = b.dma("sync", pbt[:, 0:rowsn], src,
                             waits=[("c" + name, cast_done[name]),
                                    ("aq", prep_wb_last[half])], transpose=True)
                prep_wb_last[half] = b.dma("scalar", dst, pbt[:, 0:rowsn],
                                           waits=[("sq", tcum)])

    ln_idx = [0]

    def ln_tile(x_ap, out_ap, fused_ln1, src_waits):
        k = ln_idx[0] % 4
        ln_idx[0] += 1
        s1 = stats[:, 4 * k:4 * k + 1]
        vv = stats[:, 4 * k + 1:4 * k + 2]
        mean = stats[:, 4 * k + 2:4 * k + 3]
        rstd = stats[:, 4 * k + 3:4 * k + 4]
        c_sum = b.v(lambda e: e.reduce_sum(s1, x_ap, axis=AX.X), waits=src_waits)
        c_sq = b.s(act(sqscr[:], x_ap, AF.Square, accum_out=vv), waits=src_waits)
        c1 = b.v(tsc(mean, s1, 1.0 / D, None, ALU.mult, ALU.bypass),
                 waits=[("ve", c_sum)])
        c2 = b.v(tsc(vv, vv, 1.0 / D, None, ALU.mult, ALU.bypass),
                 waits=[("se", c_sq)])
        msq = stats[:, 16:17]
        c3 = b.v(ttt(msq, mean, mean, ALU.mult), waits=[("ve", c1), ("ve", c2)])
        c4 = b.v(ttt(vv, vv, msq, ALU.subtract), waits=[("ve", c3)])
        if fused_ln1:
            lnscale, lnbias = (1.0 + EPS), fconst[:, 0:1]
        else:
            lnscale, lnbias = 1.0, fconst[:, 1:2]
        c5 = b.s(act(rstd, vv, AF.Ln, bias=lnbias, scale=lnscale),
                 waits=[("ve", c4), ("cst", consts_sq)])
        c6 = b.s(act(rstd, rstd, AF.Exp, scale=-0.5), waits=[("se", c5)])
        c7 = b.v(lambda e: e.scalar_tensor_tensor(mean, mean, -1.0, rstd,
                                                  ALU.mult, ALU.mult),
                 waits=[("se", c6)])
        c8 = b.v(tsc(out_ap, x_ap, rstd, mean, ALU.mult, ALU.add),
                 waits=[("ve", c7)])
        return c8

    x_free = [None, None]
    tfree = {"t0": None, "t1": None, "t2": None}
    bigA_gate = [None]
    bigB_gate = [None]
    wbfree = [None, None]
    sq_xn2_done = [0]
    pe_attn_last = [0]
    maxmm_last = [0]
    maxred_last = [None]

    for s in range(2):
        srow = s * S
        for i in range(16):
            par = i % 2
            xt = (xa, xb)[par]
            nt = (t0, t1)[par]
            ntn = ("t0", "t1")[par]
            ld = b.dma("sync", xt[:], x_in[srow + i * 128:srow + (i + 1) * 128, :],
                       waits=[x_free[par]])
            va = ln_tile(xt[:], nt[:], True, [("sq", ld), tfree[ntn]])
            x_free[par] = ("ve", va)
            wr = b.dma("gpsimd", xn_h[srow + i * 128:srow + (i + 1) * 128, :],
                       nt[:], waits=[("ve", va)], sem=("xn0", "xn1")[par])
            tfree[ntn] = (("xn0", "xn1")[par], wr)
            pump_prep(1)
        p1_xn = [b.cnt["xn0"], b.cnt["xn1"]]

        ve_xnT = 0
        for c in range(NCD):
            w = [("xn0", p1_xn[0]), ("xn1", p1_xn[1])]
            if c == 0 and bigA_gate[0] is not None:
                w.append(bigA_gate[0])
            b.dma("sync", bigA[:, c, :],
                  xn_h[srow:srow + S, c * 128:(c + 1) * 128],
                  waits=w, transpose=True, sem="xnt")
            pump_prep(1)
        xnt_cum = b.cnt["xnt"]
        for c in range(NCD):
            ve_xnT = b.v(tsc(bigA[:, c, :], bigA[:, c, :],
                             g16s[:, c:c + 1], b16s[:, c:c + 1],
                             ALU.mult, ALU.add),
                         waits=[("xnt", xnt_cum), ("cst", consts_sq)])

        if xnT_dbg is not None:
            for c in range(NCD):
                b.dma("gpsimd", xnT_dbg[s, :, c, :], bigA[:, c, :],
                      waits=[("ve", ve_xnT)], sem="mis")
            xnT_dbg_mis = b.cnt["mis"]
        else:
            xnT_dbg_mis = 0

        qkv_iters = []
        for wi, (W, dstT, bias_s) in enumerate(
                ((wq, qT_h, bq16s), (wk, kT_h, bk16s), (wv, vT_h, bv16s))):
            Wr = W.rearrange("(c p) n -> p c n", p=128)
            for c in range(NCD):
                qkv_iters.append((wi, Wr, dstT, bias_s, c))
        ldw_cum = [None, None]

        def p3_ldw(it):
            _, Wr_, _, _, c_ = qkv_iters[it]
            par_ = it % 2
            ldw_cum[par_] = b.dma("gpsimd", (wb0, wb1)[par_].ap(),
                                  Wr_[:, :, c_ * 128:(c_ + 1) * 128],
                                  waits=[wbfree[par_]],
                                  sem=("wb0s", "wb1s")[par_])

        p3_ldw(0)
        psA_half_free = [psA_free, psA_free]
        pending = [None]

        def p3_flush_max():
            if pending[0] is None:
                return
            wi_, c_, et_, etn_, sv0, sv1, dstT_ = pending[0]
            pending[0] = None
            vsq = b.v(lambda e, o=sqscr[:], i=et_[:]: e.scalar_tensor_tensor(
                o, i, 1.0, i, ALU.mult, ALU.mult),
                waits=[("se", sv0), ("se", sv1), ("pe", maxmm_last[0])])
            pemax = None
            for n in range(4):
                w = [("ve", vsq)]
                if n == 0:
                    w.append(("cst", consts_sq))
                    if maxred_last[0] is not None:
                        w.append(maxred_last[0])
                pemax = b.t(mmf(psB[:, n * 512:(n + 1) * 512], ones_tile[:],
                                sqscr[:, n * 512:(n + 1) * 512], True, True),
                            waits=w)
            maxmm_last[0] = pemax
            dcol = hstats[:, wi_ * 16 + c_:wi_ * 16 + c_ + 1]
            vred = b.v(lambda e, o=dcol: e.reduce_max(o, psB[:], axis=AX.X),
                       waits=[("pe", pemax)])
            maxred_last[0] = ("ve", vred)
            esem = "et0" if etn_ == "t0" else "et1"
            wr = b.dma("gpsimd", dstT_[s * D + c_ * 128:s * D + (c_ + 1) * 128, :],
                       et_[:], waits=[("se", sv0), ("se", sv1), ("ve", vsq)],
                       sem=esem)
            tfree[etn_] = (esem, wr)

        for it, (wi, Wr, dstT, bias_s, c) in enumerate(qkv_iters):
            par = it % 2
            wbt = (wb0, wb1)[par]
            ldw = ldw_cum[par]
            if it + 1 < len(qkv_iters):
                p3_ldw(it + 1)
            et = (t0, t1)[par]
            etn = ("t0", "t1")[par]
            sevs = []
            pe_last = None
            for half in range(2):
                h0 = half * 1024
                for dc in range(NCD):
                    for n in range(2):
                        w = []
                        if dc == 0 and n == 0:
                            w = [(("wb0s", "wb1s")[par], ldw), ("ve", ve_xnT),
                                 psA_half_free[half]]
                        pe_last = b.t(mmf(psA[:, h0 + n * 512:h0 + (n + 1) * 512],
                                          wbt[:, dc, :],
                                          bigA[:, dc, h0 + n * 512:h0 + (n + 1) * 512],
                                          dc == 0, dc == NCD - 1), waits=w)
                sev = b.s(act(et[:, h0:h0 + 1024], psA[:, h0:h0 + 1024],
                              AF.Identity, bias=bias_s[:, c:c + 1]),
                          waits=[("pe", pe_last), tfree[etn],
                                 ("cst", consts_sq)])
                psA_half_free[half] = ("se", sev)
                sevs.append(sev)
                if half == 0:
                    p3_flush_max()
            wbfree[par] = ("pe", pe_last)
            if wi < 2:
                pending[0] = (wi, c, et, etn, sevs[0], sevs[1], dstT)
            else:
                esem = ("et0", "et1")[par]
                wr = b.dma("gpsimd", dstT[s * D + c * 128:s * D + (c + 1) * 128, :],
                           et[:], waits=[("se", sevs[0]), ("se", sevs[1])],
                           sem=esem)
                tfree[etn] = (esem, wr)
            pump_prep(1)
        p3_flush_max()
        q2 = hstats[:, 0:16]
        k2 = hstats[:, 16:32]
        negc_all = hstats[:, 32:48]
        sl1 = b.s(act(q2, q2, AF.Ln, bias=fconst[:, 2:3]),
                  waits=[maxred_last[0], ("cst", consts_sq)])
        sl2 = b.s(act(k2, k2, AF.Ln, bias=fconst[:, 2:3]), waits=[("se", sl1)])
        v3 = b.v(ttt(negc_all, q2, k2, ALU.add), waits=[("se", sl2)])
        s4 = b.s(act(negc_all, negc_all, AF.Exp, scale=0.5, bias=fconst[:, 3:4]),
                 waits=[("ve", v3)])
        negc_done = b.s(act(negc_all, negc_all, AF.Identity, scale=-1.0),
                        waits=[("se", s4)])
        p3_pe = b.cnt["pe"]
        p3_et = [b.cnt["et0"], b.cnt["et1"]]

        for tt in range(NCD):
            w = [("et0", p3_et[0]), ("et1", p3_et[1])]
            if tt == 0 and bigB_gate[0] is not None:
                w.append(bigB_gate[0])
            b.dma("sync", bigB[:, tt, :],
                  vT_h[s * D:s * D + D, tt * 128:(tt + 1) * 128],
                  waits=w, transpose=True, sem="vsq0")
        vseq_sq = b.cnt["vsq0"]

        qh_free = [("pe", p3_pe), ("pe", p3_pe)]
        kh_free = [("pe", p3_pe), ("pe", p3_pe)]
        slot_free = [None] * 16
        sfree = [psA_half_free[0], psA_half_free[0],
                 psA_half_free[1], psA_half_free[1]]
        ofree = [maxred_last[0], maxred_last[0]]
        sumfree = [maxred_last[0], maxred_last[0]]
        rtfree = [None, None]
        evfree = [None, None]
        for h in range(NH):
            par = h % 2
            qt = qh[par]
            kt = kh[par]
            qksem = ("qk0", "qk1")[par]
            b.dma("sync", qt, qT_h[s * D + h * 128:s * D + (h + 1) * 128, :],
                  waits=[qh_free[par],
                         ("vsq0", vseq_sq) if h == 0 else None,
                         ("et0", p3_et[0]) if h < 2 else None,
                         ("et1", p3_et[1]) if h < 2 else None,
                         ("mis", xnT_dbg_mis) if h < 2 else None], sem=qksem)
            ldqk = b.dma("sync", kt,
                         kT_h[s * D + h * 128:s * D + (h + 1) * 128, :],
                         waits=[kh_free[par],
                                ("et0", p3_et[0]) if h < 2 else None,
                                ("et1", p3_et[1]) if h < 2 else None],
                         sem=qksem)
            negc = negc_all[:, h:h + 1]
            pe_S_last = 0
            jobs = []
            for Q in range(4):
                for j in range(4 * Q + 4):
                    jobs.append((Q, j))
            pdeps = {}
            po = {"pe_O": None}

            def emit_avsum(idx):
                Q, j = jobs[idx]
                qpar = Q % 2
                nj = 4 * Q + 4
                o_ps = psB[:, qpar * 512:(qpar + 1) * 512]
                sum_ps = psB[:, (2 + qpar) * 512:(3 + qpar) * 512]
                pdep = pdeps.pop(idx)
                w = [pdep]
                if j == 0:
                    w += [ofree[qpar], ("vsq0", vseq_sq)]
                pe_O = b.t(mmf(o_ps, bigB[:, j, h * 128:(h + 1) * 128],
                               pT_slot(j), j == 0, j == nj - 1), waits=w)
                w = [pdep] + ([sumfree[qpar], ("cst", consts_sq)]
                              if j == 0 else [])
                pe_sum = b.t(mmf(sum_ps, ones_tile[:], pT_slot(j),
                                 j == 0, j == nj - 1), waits=w)
                slot_free[j] = ("pe", pe_sum)
                if j == nj - 1:
                    rtt = (rt0, rt1)[qpar]
                    ln_s = b.s(act(rtt[:], sum_ps, AF.Ln),
                               waits=[("pe", pe_sum), rtfree[qpar]])
                    sumfree[qpar] = ("se", ln_s)
                    ex_s = b.s(act(rtt[:], rtt[:], AF.Exp, scale=-1.0),
                               waits=[("se", ln_s)])
                    evt = (ev0, ev1)[qpar]
                    vev = b.v(ttt(evt[:], o_ps, rtt[:], ALU.mult),
                              waits=[("se", ex_s), ("pe", pe_O), evfree[qpar]])
                    ofree[qpar] = ("ve", vev)
                    rtfree[qpar] = ("ve", vev)
                    wr = b.dma("gpsimd",
                               avT_h[s * D + h * 128:s * D + (h + 1) * 128,
                                     Q * 512:(Q + 1) * 512], evt[:],
                               waits=[("ve", vev)], sem=("av0", "av1")[qpar])
                    evfree[qpar] = (("av0", "av1")[qpar], wr)

            for idx, (Q, j) in enumerate(jobs):
                bank = idx % 4
                s_ps1 = psA[:, bank * 512:(bank + 1) * 512]
                w = [sfree[bank]]
                if idx == 0:
                    w.append((qksem, ldqk))
                mmS = b.t(mmf(s_ps1, kt[:, j * 128:(j + 1) * 128],
                              qt[:, Q * 512:(Q + 1) * 512], True, True),
                          waits=w)
                pe_S_last = mmS
                ex = b.s(act(pT_slot(j), s_ps1, AF.Exp, bias=negc,
                             scale=INV_SQRT_DH),
                         waits=[("pe", mmS), slot_free[j],
                                ("se", negc_done) if idx == 0 and h == 0
                                else None])
                sfree[bank] = ("se", ex)
                pdep = ("se", ex)
                if j >= 4 * Q:
                    r = j - 4 * Q
                    vm = b.v(ttt(pT_slot(j), pT_slot(j),
                                 masks[:, r, :], ALU.mult),
                             waits=[("se", ex), ("cst", consts_sq)])
                    pdep = ("ve", vm)
                pdeps[idx] = pdep
                if idx >= 2:
                    emit_avsum(idx - 2)
            emit_avsum(len(jobs) - 2)
            emit_avsum(len(jobs) - 1)
            qh_free[par] = ("pe", pe_S_last)
            kh_free[par] = ("pe", pe_S_last)
            pump_prep(2)
        p4_av = [b.cnt["av0"], b.cnt["av1"]]
        pe_attn_last[0] = b.cnt["pe"]

        for din in range(NCD):
            w = [("av0", p4_av[0]), ("av1", p4_av[1])]
            if din == 0:
                w.append(("pe", pe_attn_last[0]))
            b.dma("sync", bigA[:, din, :],
                  avT_h[s * D + din * 128:s * D + (din + 1) * 128, :], waits=w,
                  sem="avl")
        avT_sq = b.cnt["avl"]
        Wor = wo.rearrange("(c p) n -> p c n", p=128)

        def p5_ldw(c_):
            par_ = c_ % 2
            ldw_cum[par_] = b.dma("gpsimd", (wb0, wb1)[par_].ap(),
                                  Wor[:, :, c_ * 128:(c_ + 1) * 128],
                                  waits=[wbfree[par_]],
                                  sem=("wb0s", "wb1s")[par_])

        p5_ldw(0)
        for c in range(NCD):
            par = c % 2
            wbt = (wb0, wb1)[par]
            ldw = ldw_cum[par]
            if c + 1 < NCD:
                p5_ldw(c + 1)
            ps = (psA, psB)[par]
            ps_gate = psA_free if par == 0 else psB_free
            if c < 2:
                ps_gate_extra = [sfree[0], sfree[1], sfree[2], sfree[3],
                                 ofree[0], ofree[1], sumfree[0], sumfree[1]]
            else:
                ps_gate_extra = []
            pe_last = None
            for dc in range(NCD):
                for n in range(4):
                    w = []
                    if dc == 0 and n == 0:
                        w = [(("wb0s", "wb1s")[par], ldw), ("avl", avT_sq),
                             ps_gate] + ps_gate_extra
                    pe_last = b.t(mmf(ps[:, n * 512:(n + 1) * 512], wbt[:, dc, :],
                                      bigA[:, dc, n * 512:(n + 1) * 512],
                                      dc == 0, dc == NCD - 1), waits=w)
            wbfree[par] = ("pe", pe_last)
            et = (t0, t1)[par]
            etn = ("t0", "t1")[par]
            sev = b.s(act(et[:], ps[:], AF.Identity), waits=[("pe", pe_last), tfree[etn]])
            if par == 0:
                psA_free = ("se", sev)
            else:
                psB_free = ("se", sev)
            wr = b.dma("gpsimd", attnT_h[s * D + c * 128:s * D + (c + 1) * 128, :],
                       et[:], waits=[("se", sev)], sem=("et0", "et1")[par])
            tfree[etn] = (("et0", "et1")[par], wr)
            pump_prep(2)
        p5a_et = [b.cnt["et0"], b.cnt["et1"]]
        p5a_pe = b.cnt["pe"]
        bigA_gate[0] = ("pe", p5a_pe)
        bigB_gate[0] = ("pe", pe_attn_last[0])

        t2_free = tfree["t2"]
        for i in range(16):
            par = i % 2
            zt = (t0, t1)[par]
            ztn = ("t0", "t1")[par]
            tr = b.dma("sync", zt[:],
                       attnT_h[s * D:s * D + D, i * 128:(i + 1) * 128],
                       waits=[("et0", p5a_et[0]), ("et1", p5a_et[1]),
                              tfree[ztn]], transpose=True)
            ldx = b.dma("sync", (xa, xb)[par][:],
                        x_in[srow + i * 128:srow + (i + 1) * 128, :],
                        waits=[x_free[par]])
            v1 = b.v(ttt(zt[:], zt[:], bo_tile[:], ALU.add),
                     waits=[("sq", tr), ("ve", ve_bo)])
            va = ln_tile(zt[:], f0[:], False, [("ve", v1), tfree["f0"] if "f0" in tfree else None])
            tfree[ztn] = ("ve", va)
            v2 = b.v(ttt(f0[:], f0[:], (xa, xb)[par][:], ALU.add),
                     waits=[("ve", va), ("sq", ldx)])
            x_free[par] = ("ve", v2)
            wr1 = b.dma("gpsimd", x1_h[srow + i * 128:srow + (i + 1) * 128, :],
                        f0[:], waits=[("ve", v2)], sem="xf")
            vb = ln_tile(f0[:], t2[:], False, [("ve", v2), t2_free])
            wr2 = b.dma("gpsimd", xn2_h[srow + i * 128:srow + (i + 1) * 128, :],
                        t2[:], waits=[("ve", vb)], sem="xf")
            tfree["f0"] = ("xf", wr2)
            t2_free = ("xf", wr2)
            pump_prep(1)
        tfree["t2"] = t2_free
        sq_xn2_done[0] = b.cnt["xf"]

    pump_prep(len(prep_ops))
    prep_aq = b.cnt["aq"]
    silu_t = sqscr
    silu_free = [0]
    wl_free = [None, None]
    gated_free = [tfree["t0"], tfree["t1"]]
    ve_gate_last = [0]
    pe_p6_first_gate = [bigA_gate[0]]
    p6_bigA_sq = [0]
    for H in range(2):
        trow = H * 2048
        tr_last = 0
        for c in range(NCD):
            w = [("xf", sq_xn2_done[0])]
            if c == 0:
                if H == 0:
                    w += pe_p6_first_gate
                else:
                    w += [("pe", b.cnt["pe"])]
            tr_last = b.dma("sync", bigA[:, c, :],
                            xn2_h[trow:trow + 2048, c * 128:(c + 1) * 128],
                            waits=w, transpose=True, sem="xnt")
        WgTr = wgT_h.rearrange("(c p) h -> p c h", p=128)
        WuTr = wuT_h.rearrange("(c p) h -> p c h", p=128)
        for hp in range(32):
            par = hp % 2
            w = [("aq", prep_aq), wl_free[par]]
            if hp == 0 and H == 0:
                w.append(bigB_gate[0])
            ldg = b.dma("sync", wlg[par],
                        WgTr[:, :, hp * 256:(hp + 1) * 256], waits=w)
            ldu = b.dma("sync", wlu[par],
                        WuTr[:, :, hp * 256:(hp + 1) * 256])
            pe_wl_last = None
            for half in range(2):
                h = 2 * hp + half
                hs = half * 128
                peg = None
                for dc in range(NCD):
                    for n in range(4):
                        w = []
                        if dc == 0 and n == 0:
                            w = [("sq", ldu if half == 0 else 0),
                                 ("xnt", tr_last), psA_free]
                        peg = b.t(mmf(psA[:, n * 512:(n + 1) * 512],
                                      wlg[par][:, dc, hs:hs + 128],
                                      bigA[:, dc, n * 512:(n + 1) * 512],
                                      dc == 0, dc == NCD - 1), waits=w)
                sg = b.s(act(silu_t[:], psA[:], AF.Silu),
                         waits=[("pe", peg), ("ve", silu_free[0])])
                psA_free = ("se", sg)
                peu = None
                for dc in range(NCD):
                    for n in range(4):
                        w = [psB_free] if (dc == 0 and n == 0) else []
                        peu = b.t(mmf(psB[:, n * 512:(n + 1) * 512],
                                      wlu[par][:, dc, hs:hs + 128],
                                      bigA[:, dc, n * 512:(n + 1) * 512],
                                      dc == 0, dc == NCD - 1), waits=w)
                pe_wl_last = peu
                gt = (t0, t1)[half]
                vg = b.v(ttt(gt[:], psB[:], silu_t[:], ALU.mult),
                         waits=[("pe", peu), ("se", sg), gated_free[half]])
                silu_free[0] = vg
                psB_free = ("ve", vg)
                wr = b.dma("gpsimd", gatedT_h[h * 128:(h + 1) * 128,
                                              trow:trow + 2048], gt[:],
                           waits=[("ve", vg)], sem=("gt0", "gt1")[half])
                gated_free[half] = (("gt0", "gt1")[half], wr)
            wl_free[par] = ("pe", pe_wl_last)
    p6_gt = [b.cnt["gt0"], b.cnt["gt1"]]
    p6_pe = b.cnt["pe"]

    stack2.close()
    accum = sb("accum", [128, 4, 2048], F32)
    WdF = wdT_h.rearrange("(F p) d -> F p d", p=128)
    GTr = gatedT_h.rearrange("(G hc p) t -> G p hc t", G=8, hc=8, p=128)
    wd_slot_free = [("pe", p6_pe)] * 16
    wd_slot_cum = [0] * 16
    gl_bufs = [bigB[:, 0:8, 1024:1536], bigB[:, 8:16, 1024:1536],
               bigB[:, 0:8, 1536:2048], bigB[:, 8:16, 1536:2048]]
    gl_free = [bigB_gate[0]] * 4
    gl_cum = [0] * 4
    blocks = [(e, gi) for e in range(8) for gi in range(8)]

    def p7_load_block(bidx):
        if bidx >= len(blocks):
            return
        e_, gi_ = blocks[bidx]
        bi = bidx % 4
        gl_cum[bi] = b.dma("sync", gl_bufs[bi],
                           GTr[gi_][:, :, e_ * 512:e_ * 512 + 512],
                           waits=[gl_free[bi], ("gt0", p6_gt[0]),
                                  ("gt1", p6_gt[1])])
        for hc in range(8):
            sl = (gi_ * 8 + hc) % 16
            wd_slot_cum[sl] = b.dma("scalar", bigA[:, sl, :], WdF[gi_ * 8 + hc],
                                    waits=[wd_slot_free[sl], ("aq", prep_aq)])

    p7_load_block(0)
    p7_load_block(1)
    acc_free = [None, None, None, None]
    f0_free = tfree["f0"]
    out_sq_last = 0
    for bidx, (e, gi) in enumerate(blocks):
        glb = gl_bufs[bidx % 4]
        pe_by_hc = [None] * 8
        pe_g_last = None
        for t in range(4):
            ps = (psA, psB)[t % 2]
            ps_gate = psA_free if t % 2 == 0 else psB_free
            for hc in range(8):
                sl = (gi * 8 + hc) % 16
                for n in range(4):
                    w = []
                    if n == 0:
                        w.append(("aq", wd_slot_cum[sl]))
                        if hc == 0:
                            w += [("sq", gl_cum[bidx % 4]), ps_gate]
                            if gi == 0:
                                w.append(acc_free[t])
                    pet = b.t(mmf(ps[:, n * 512:(n + 1) * 512],
                                  glb[:, hc, t * 128:(t + 1) * 128],
                                  bigA[:, sl, n * 512:(n + 1) * 512],
                                  hc == 0, hc == 7), waits=w)
                pe_by_hc[hc] = pet
            pe_g_last = pet
            if gi == 0:
                vacc = b.v(lambda e_, o=accum[:, t, :], i=ps[:]: e_.tensor_copy(o, i),
                           waits=[("pe", pet)])
            else:
                vacc = b.v(ttt(accum[:, t, :], accum[:, t, :], ps[:], ALU.add),
                           waits=[("pe", pet)])
            if t % 2 == 0:
                psA_free = ("ve", vacc)
            else:
                psB_free = ("ve", vacc)
            acc_free[t] = ("ve", vacc)
        for hc in range(8):
            wd_slot_free[(gi * 8 + hc) % 16] = ("pe", pe_by_hc[hc])
        gl_free[bidx % 4] = ("pe", pe_g_last)
        p7_load_block(bidx + 2)
        if gi == 7:
            for t in range(4):
                par = t % 2
                row0 = e * 512 + t * 128
                ldx1 = b.dma("sync", (xa, xb)[par][:], x1_h[row0:row0 + 128, :],
                             waits=[x_free[par], ("xf", sq_xn2_done[0])])
                va = ln_tile(accum[:, t, :], f0[:], False, [acc_free[t], f0_free])
                v2 = b.v(ttt(f0[:], f0[:], (xa, xb)[par][:], ALU.add),
                         waits=[("ve", va), ("sq", ldx1)])
                x_free[par] = ("ve", v2)
                acc_free[t] = ("ve", va)
                wro = b.dma("gpsimd", out[row0:row0 + 128, :], f0[:],
                            waits=[("ve", v2)], sem="outs")
                f0_free = ("outs", wro)
                out_sq_last = wro
    b.op("gpsimd", lambda e: e.nop(), waits=[("outs", out_sq_last)])

    b.emit()
    stack.close()
    return nc


_CACHE = {}


def _get_nc():
    if "nc" not in _CACHE:
        _CACHE["nc"] = build_program()
    return _CACHE["nc"]


def kernel(L0_attn_X, L0_attn_ln_g, L0_attn_ln_b, L0_attn_Wq, L0_attn_bq,
           L0_attn_Wk, L0_attn_bk, L0_attn_Wv, L0_attn_bv, L0_attn_Wo,
           L0_attn_bo, L0_ffn_w_gate, L0_ffn_w_up, L0_ffn_w_down):
    X = np.ascontiguousarray(np.asarray(L0_attn_X, dtype=np.float32))
    B = X.shape[0]
    n_cores = 8
    per = B // n_cores

    def cols16(v):
        return np.ascontiguousarray(np.asarray(v, np.float32).reshape(NCD, 128).T)

    masks = np.zeros((4, 128, 512), np.float32)
    for r in range(4):
        for k in range(128):
            masks[r, k, 128 * r + k:] = 1.0
    masks_bf = masks.reshape(512, 512).astype(ml_dtypes.bfloat16)

    common = {
        "wq": np.ascontiguousarray(np.asarray(L0_attn_Wq, np.float32)),
        "wk": np.ascontiguousarray(np.asarray(L0_attn_Wk, np.float32)),
        "wv": np.ascontiguousarray(np.asarray(L0_attn_Wv, np.float32)),
        "wo": np.ascontiguousarray(np.asarray(L0_attn_Wo, np.float32)),
        "wg": np.ascontiguousarray(np.asarray(L0_ffn_w_gate, np.float32)),
        "wu": np.ascontiguousarray(np.asarray(L0_ffn_w_up, np.float32)),
        "wd": np.ascontiguousarray(np.asarray(L0_ffn_w_down, np.float32)),
        "g16": cols16(L0_attn_ln_g),
        "b16": cols16(L0_attn_ln_b),
        "bq16": cols16(L0_attn_bq),
        "bk16": cols16(L0_attn_bk),
        "bv16": cols16(L0_attn_bv),
        "bo_row": np.ascontiguousarray(np.asarray(L0_attn_bo, np.float32)[None, :]),
        "ones_tile": np.ones((128, 128), ml_dtypes.bfloat16),
        "ones_col": np.ones((128, 1), ml_dtypes.bfloat16),
        "ones_row_f32": np.ones((1, 128), np.float32),
        "masks": masks_bf,
        "fconst": np.tile(np.array([[EPS * EPS, EPS, 1e-30, -HALF_LN_DH]],
                                   np.float32), (128, 1)),
    }
    in_maps = []
    for c in range(n_cores):
        m = dict(common)
        m["x_in"] = np.ascontiguousarray(X[c * per:(c + 1) * per].reshape(T, D))
        in_maps.append(m)

    nc = _get_nc()
    res = run_bass_kernel_spmd(nc, in_maps, core_ids=list(range(n_cores)))
    outs = [res.results[c]["out"].reshape(per, S, D) for c in range(n_cores)]
    return np.concatenate(outs, axis=0)


if __name__ == "__main__":
    np.random.seed(0)
    nc = build_program()
    print("build ok")
